# version 03
# baseline (speedup 1.0000x reference)
import numpy as np
import concourse.bass as bass
import concourse.mybir as mybir
from concourse.bass_utils import run_bass_kernel_spmd

F32 = mybir.dt.float32
BF16 = mybir.dt.bfloat16
AF = mybir.ActivationFunctionType
ALU = mybir.AluOpType
AX = mybir.AxisListType

D = 1024
DI = 2048
H = 32
NH = 16
CONVD = 2304
DIP = 4384
T = 2048
TT = 256
NTILE = T // TT
LC = 64
NCH = TT // LC
KD = D // 128
MIP = 35
FCX = 16
FCC = 18
EPS = 1e-5

DEBUG_TAPS = []
TAP_TILE = 0
EMIT_LOG = False

NGROUPS_WIN = 12
WIN_GROUPS = [(i * 3, min(i * 3 + 3, MIP)) for i in range(NGROUPS_WIN)]



class Builder:
    def __init__(self):
        nc = bass.Bass(target_bir_lowering=False, trn_type="TRN2")
        self.nc = nc
        self.plan = {e: [] for e in ["te", "ve", "se", "gp", "sy"]}
        self.emit_log = {}
        self.evac_hist = []
        self.tail_mark = 0
        self.cnt = {}
        self.sems = {}
        self.lastw = {e: {} for e in self.plan}
        self._es_sems = []
        for k in ["te", "ve", "se", "gp", "dS", "dG", "out"]:
            cm = nc.semaphore("sem_" + k)
            sem = cm.__enter__()
            self._es_sems.append(cm)
            self.sems[k] = sem
            self.cnt[k] = 0
            for e in self.plan:
                self.lastw[e][k] = 0
        self.tap_outs = {}
        self._declare_io()
        self._declare_consts()
        self._declare_sbuf()
        ps = nc.alloc_psum_tensor("ps", [128, 4096], F32)
        self.ps = ps
        self.psb = ps.ap().bitcast(BF16)

    def emit(self, e, fn):
        self.plan[e].append(fn)

    def emit_i(self, e, key, n, fn):
        sem = self.sems[key]
        def g(eng, fn=fn, sem=sem, n=n):
            fn(eng).then_inc(sem, n)
        self.plan[e].append(g)
        self.cnt[key] += n
        if EMIT_LOG:
            import sys
            lines = []
            f = sys._getframe(1)
            while f is not None and len(lines) < 4:
                if f.f_code.co_filename.endswith("kernel.py"):
                    lines.append(f.f_lineno)
                f = f.f_back
            self.emit_log.setdefault(e, []).append(tuple(lines))
        return self.cnt[key]

    def wait(self, e, key, val=None):
        v = self.cnt[key] if val is None else val
        assert v <= self.cnt[key], f"wait ahead of emitted count for {key}"
        if v <= self.lastw[e][key]:
            return
        self.lastw[e][key] = v
        sem = self.sems[key]
        self.plan[e].append(lambda eng, sem=sem, v=v: eng.wait_ge(sem, v))

    def barrier(self, consumers=("te", "ve", "se", "gp", "sy"),
                producers=("te", "ve", "se", "gp")):
        for e in consumers:
            for k in producers:
                if self.cnt[k] > 0:
                    self.wait(e, k)

    def dma_sy(self, dst, src):
        return self.emit_i("sy", "dS", 16, lambda eng, d=dst, s=src: eng.dma_start(out=d, in_=s))

    def dma_gp(self, dst, src):
        return self.emit_i("gp", "dG", 16, lambda eng, d=dst, s=src: eng.dma_start(out=d, in_=s))

    def te(self, fn):
        return self.emit_i("te", "te", 1, fn)

    def ve(self, fn):
        return self.emit_i("ve", "ve", 1, fn)

    def ve_reduce(self, fn):
        v = self.emit_i("ve", "ve", 1, fn)
        sem = self.sems["ve"]
        self.plan["ve"].append(lambda eng, sem=sem, v=v: eng.wait_ge(sem, v))
        self.lastw["ve"]["ve"] = max(self.lastw["ve"]["ve"], v)
        return v

    def se(self, fn):
        return self.emit_i("se", "se", 1, fn)

    def gp(self, fn):
        return self.emit_i("gp", "gp", 1, fn)

    def bank(self, i, lo=0, n=512, p0=0, np_=128):
        return self.ps[p0:p0 + np_, i * 512 + lo: i * 512 + lo + n]

    def tap(self, name, ap, shape, dtype=F32):
        if name not in DEBUG_TAPS or name in self.tap_outs:
            return
        t = self.nc.dram_tensor("tap_" + name, shape, dtype, kind="ExternalOutput").ap()
        self.tap_outs[name] = t
        for key in ["te", "ve", "se", "gp"]:
            self.wait("sy", key)
        self.emit_i("sy", "out", 16, lambda eng, t=t, a=ap: eng.dma_start(out=t, in_=a))
        for e in ["te", "ve", "se", "gp"]:
            self.wait(e, "out", self.cnt["out"])

    def _declare_io(self):
        nc = self.nc
        A = lambda *a, **k: nc.dram_tensor(*a, **k).ap()
        self.d_u = A("u", [T, D], F32, kind="ExternalInput")
        self.d_winT = A("winT", [D, DIP], F32, kind="ExternalInput")
        self.d_woutT = A("woutT", [DI, D], F32, kind="ExternalInput")
        self.d_convw = A("convw", [CONVD, 4], F32, kind="ExternalInput")
        self.d_convb = A("convb", [128, FCC], F32, kind="ExternalInput")
        self.d_dtb = A("dtb", [H, 1], F32, kind="ExternalInput")
        self.d_alog = A("alog", [128, H], F32, kind="ExternalInput")
        self.d_dfold = A("dfold", [128, NH], F32, kind="ExternalInput")
        self.d_nw = A("nw", [128, 16], F32, kind="ExternalInput")
        self.d_sig = A("sig", [64, D], F32, kind="ExternalInput")
        self.d_knots = A("knots", [1, 8], F32, kind="ExternalInput")
        self.d_knotsc = A("knotsc", [8, 1], F32, kind="ExternalInput")
        self.d_temp = A("temp", [128, 1], F32, kind="ExternalInput")
        self.d_cw1 = A("cw1", [D, 256], F32, kind="ExternalInput")
        self.d_cb1 = A("cb1", [128, 2], F32, kind="ExternalInput")
        self.d_cw2 = A("cw2", [256, 2], F32, kind="ExternalInput")
        self.d_cb2 = A("cb2", [2, 1], F32, kind="ExternalInput")
        self.d_w1all = A("w1all", [2, 1024], F32, kind="ExternalInput")
        self.d_b1all = A("b1all", [128, 8], F32, kind="ExternalInput")
        self.d_w2bd = A("w2bd", [1024, 64], F32, kind="ExternalInput")
        self.d_b2all = A("b2all", [64, 1], F32, kind="ExternalInput")
        self.d_dirs = A("dirs", [64, D], F32, kind="ExternalInput")
        self.d_gam = A("gam", [128, D], F32, kind="ExternalInput")
        self.d_bet = A("bet", [128, D], F32, kind="ExternalInput")
        self.d_os = A("os", [64, 1], F32, kind="ExternalInput")
        self.d_out = A("out", [T, D], F32, kind="ExternalOutput")
        self.d_winT_bf = A("winT_bf", [D, DIP], BF16, kind="Internal")
        self.d_woutT_bf = A("woutT_bf", [DI, D], BF16, kind="Internal")

    def _declare_consts(self):
        nc = self.nc
        eye64 = np.eye(64, dtype=np.float32)
        self.c_ident = nc.inline_tensor(np.eye(128, dtype=np.float32), "c_ident")
        self.c_identb = nc.inline_tensor(np.eye(128, dtype=np.float32), "c_identb_src")
        lt = np.tril(np.ones((64, 64), np.float32)).T
        self.c_lt = nc.inline_tensor(lt, "c_lt")
        ih0 = np.zeros((64, 128), np.float32); ih0[:, 0:64] = eye64
        ih1 = np.zeros((64, 128), np.float32); ih1[:, 64:128] = eye64
        self.c_ih0 = nc.inline_tensor(ih0, "c_ih0")
        self.c_ih1 = nc.inline_tensor(ih1, "c_ih1")
        self.c_nih0 = nc.inline_tensor(-ih0, "c_nih0")
        self.c_nih1 = nc.inline_tensor(-ih1, "c_nih1")
        oh0 = np.zeros((64, 128), np.float32); oh0[:, 0:64] = 1.0
        oh1 = np.zeros((64, 128), np.float32); oh1[:, 64:128] = 1.0
        self.c_oh0 = nc.inline_tensor(oh0, "c_oh0")
        self.c_oh1 = nc.inline_tensor(oh1, "c_oh1")
        self.c_onesall = nc.inline_tensor(np.ones((64, 128), np.float32), "c_onesall")
        idsh = np.zeros((128, 64), np.float32)
        idsh[64 + np.arange(64), np.arange(64)] = 1.0
        self.c_idsh = nc.inline_tensor(idsh, "c_idsh_src")
        ist = np.concatenate([eye64, eye64], axis=1)
        self.c_ist = nc.inline_tensor(ist, "c_ist_src")
        self.c_istsq = nc.inline_tensor(np.tile(eye64, (2, 2)).astype(np.float32), "c_istsq")
        m01 = np.concatenate([lt, lt], axis=0).astype(np.float32)
        self.c_m01 = nc.inline_tensor(m01, "c_m01")
        self.c_ri64 = nc.inline_tensor(
            np.broadcast_to(64.0 - np.arange(64, dtype=np.float32), (128, 64)).copy(), "c_ri64")
        self.c_ri8 = nc.inline_tensor(
            np.broadcast_to(8.0 - np.arange(8, dtype=np.float32), (128, 8)).copy(), "c_ri8")
        self.c_ones1 = nc.inline_tensor(np.ones((1, 128), np.float32), "c_ones1")
        cm = np.zeros((8, 8), np.float32)
        cm[0, 0] = -1.0; cm[1, 0] = 1.0
        for j in range(1, 7):
            cm[j + 1, j] += 1.0; cm[j, j] += -2.0; cm[j - 1, j] += 1.0
        self.c_cm = nc.inline_tensor(cm, "c_cm")
        g8 = np.zeros((64, 8), np.float32)
        g8[np.arange(64), np.arange(64) // 8] = 1.0
        self.c_g8 = nc.inline_tensor(g8, "c_g8")

    def _declare_sbuf(self):
        SB = self.nc.alloc_sbuf_tensor
        self.s_ident = SB("s_ident", [128, 128], F32)
        self.s_identb = SB("s_identb", [128, 128], BF16)
        self.s_lt = SB("s_lt", [64, 64], F32)
        self.s_ih0 = SB("s_ih0", [64, 128], F32)
        self.s_ih1 = SB("s_ih1", [64, 128], F32)
        self.s_nih0 = SB("s_nih0", [64, 128], F32)
        self.s_nih1 = SB("s_nih1", [64, 128], F32)
        self.s_oh0c = SB("s_oh0c", [64, 128], F32)
        self.s_oh1c = SB("s_oh1c", [64, 128], F32)
        self.s_onesall = SB("s_onesall", [64, 128], F32)
        self.s_ist = SB("s_ist", [64, 128], BF16)
        self.s_idsh = SB("s_idsh", [128, 64], BF16)
        self.s_istsq = SB("s_istsq", [128, 128], F32)
        self.s_m01 = SB("s_m01", [128, 64], F32)
        self.s_b2 = SB("s_b2", [128, 128], BF16)
        self.s_ri64 = SB("s_ri64", [128, 64], F32)
        self.s_ri8 = SB("s_ri8", [128, 8], F32)
        self.s_ones1 = SB("s_ones1", [1, 128], F32)
        self.s_g8 = SB("s_g8", [64, 8], F32)
        self.s_cm = SB("s_cm", [8, 8], F32)
        self.s_knc = SB("s_knc", [8, 1], F32)
        self.s_convb = SB("s_convb", [128, FCC], F32)
        self.s_dtb = SB("s_dtb", [H, 1], F32)
        self.s_negea = SB("s_negea", [128, H], F32)
        self.s_dfold = SB("s_dfold", [128, NH], F32)
        self.s_nw = SB("s_nw", [128, 16], F32)
        self.s_gam = SB("s_gam", [128, D], F32)
        self.s_bet = SB("s_bet", [128, D], F32)
        self.s_os = SB("s_os", [64, 1], F32)
        self.s_itemp = SB("s_itemp", [128, 1], F32)
        self.s_nitemp = SB("s_nitemp", [128, 1], F32)
        self.s_ccol = SB("s_ccol", [128, 8], F32)
        self.s_k0 = SB("s_k0", [128, 1], F32)
        self.s_kr = SB("s_kr", [1, 8], F32)
        self.s_t8 = SB("s_t8", [1, 8], F32)
        self.s_crow = SB("s_crow", [1, 8], F32)
        self.s_cwcol = SB("s_cwcol", [128, FCC * 4], F32)
        self.s_sig72 = SB("s_sig72", [128, KD * 72], BF16)
        self.s_cw1 = SB("s_cw1", [128, KD * 256], BF16)
        self.s_cw2 = SB("s_cw2", [128, 4], BF16)
        self.s_cb1 = SB("s_cb1", [128, 2], F32)
        self.s_cb2 = SB("s_cb2", [2, 1], F32)
        self.s_w1all = SB("s_w1all", [2, 1024], BF16)
        self.s_b1all = SB("s_b1all", [128, 8], F32)
        self.s_w2bd = SB("s_w2bd", [128, 8 * 64], BF16)
        self.s_b2os = SB("s_b2os", [64, 1], F32)
        self.s_dirs = SB("s_dirs", [64, D], BF16)
        self.s_convdiag = SB("s_convdiag", [128, FCC * 4 * 128], BF16)
        self.s_wo = [SB(f"s_wo{i}", [128, 16 * 128], BF16) for i in range(2)]
        self.s_prev = SB("s_prev", [128, DI], F32)
        self.s_prevb = SB("s_prevb", [128, DI], BF16)
        self.s_xbctail = SB("s_xbctail", [128, FCC * 3], BF16)
        self.s_win = [SB(f"s_win{i}", [128, KD * 3 * 128], BF16) for i in range(3)]
        self.s_x = [SB(f"s_x{i}", [128, 2 * D], F32) for i in range(2)]
        self.s_ln = SB("s_ln", [128, 2 * D], BF16)
        self.s_lnfm = SB("s_lnfm", [128, KD * TT], BF16)
        self.s_zfm = SB("s_zfm", [128, FCX * TT], BF16)
        self.s_xbc = SB("s_xbc", [128, FCC * (TT + 3)], BF16)
        self.s_xfm = SB("s_xfm", [128, FCX * TT], BF16)
        self.s_bfm = SB("s_bfm", [128, TT], BF16)
        self.s_cfm = SB("s_cfm", [128, TT], BF16)
        self.s_dtraw = SB("s_dtraw", [H, TT], F32)
        self.s_dtfm = SB("s_dtfm", [H, TT], F32)
        self.s_sp1 = SB("s_sp1", [H, TT], F32)
        self.s_yfm = SB("s_yfm", [128, 16 * TT], BF16)
        self.s_attn = SB("s_attn", [128, 2 * D], F32)
        self.s_dt0 = SB("s_dt0", [64, H], F32)
        self.s_lndt0 = SB("s_lndt0", [64, H], F32)
        self.s_da0 = SB("s_da0", [64, H], F32)
        self.s_ca0 = SB("s_ca0", [64, H], F32)
        self.s_d10 = SB("s_d10", [64, H], F32)
        self.s_blob = SB("s_blob", [128, 16], F32)
        self.s_cbt2 = SB("s_cbt2", [128, 64], F32)
        self.s_e2f = SB("s_e2f", [128, NH], F32)
        self.s_e3f = SB("s_e3f", [128, NH], F32)
        self.s_texp = SB("s_texp", [128, NH], F32)
        self.s_cdec = SB("s_cdec", [128, H], F32)
        self.s_diff = SB("s_diff", [128, 1024], F32)
        self.s_mblob = SB("s_mblob", [128, 1024], BF16)
        self.s_btm = SB("s_btm", [128, 128], BF16)
        self.s_xtm = SB("s_xtm", [128, 1024], BF16)
        self.s_xs = SB("s_xs", [128, 1024], BF16)
        self.s_xd = SB("s_xd", [128, 1024], F32)
        self.s_zc = SB("s_zc", [128, 1024], F32)
        self.s_yc = SB("s_yc", [128, 1024], F32)
        self.s_yn = SB("s_yn", [128, 1024], BF16)
        self.s_ss = SB("s_ss", [128, 1], F32)
        self.s_rs = SB("s_rs", [128, 1], F32)
        self.s_stt = SB("s_stt", [128, 1024], F32)
        self.s_scr = SB("s_scr", [128, 1024], F32)
        self.s_scr2 = SB("s_scr2", [128, 1024], F32)
        self.s_cvals = SB("s_cvals", [128, 13], F32)
        self.s_m1 = SB("s_m1", [128, 4], F32)
        self.s_msnap = SB("s_msnap", [128, 4], F32)
        self.s_csnap = SB("s_csnap", [128, 2 * D], F32)
        self.s_m2 = SB("s_m2", [128, 4], F32)
        self.s_mean = SB("s_mean", [128, 4], F32)
        self.s_rstd = SB("s_rstd", [128, 4], F32)
        self.s_hfm = SB("s_hfm", [128, 2 * TT], BF16)
        self.s_comp = SB("s_comp", [2, TT], BF16)
        self.s_hh = SB("s_hh", [128, 8 * TT], BF16)
        self.s_mag = SB("s_mag", [64, TT], F32)
        self.s_scores = SB("s_scores", [128, 2 * 72], F32)
        self.s_gates = SB("s_gates", [128, 72], F32)
        self.s_gates2 = SB("s_gates2", [128, 72], F32)
        self.s_oh = SB("s_oh", [128, 2 * 64], BF16)
        self.s_ohfm = SB("s_ohfm", [64, TT], BF16)
        self.s_rt = SB("s_rt", [128, 72], F32)
        self.s_rt2 = SB("s_rt2", [128, 72], F32)
        self.s_rc1 = SB("s_rc1", [128, 1], F32)

    CONST_VALS = [1e-5, -0.5, -1.0, 1.0 / DI, -2.0, -3.0, -4.0, -5.0, -6.0, 1.0, 1.0 / D, 7.0, 30.0]

    def init_phase(self):
        s = self
        for i, v in enumerate(self.CONST_VALS):
            s.ve(lambda e, i=i, v=v: e.memset(s.s_cvals[:, i:i + 1], v))
            try:
                s.nc.const_aps.register(F32, v, s.s_cvals[:, i:i + 1])
            except Exception:
                pass
        for dst, src in [
            (s.s_ident[:], s.c_ident.ap()[:]), (s.s_lt[:], s.c_lt.ap()[:]),
            (s.s_ih0[:], s.c_ih0.ap()[:]), (s.s_ih1[:], s.c_ih1.ap()[:]),
            (s.s_nih0[:], s.c_nih0.ap()[:]), (s.s_nih1[:], s.c_nih1.ap()[:]),
            (s.s_oh0c[:], s.c_oh0.ap()[:]), (s.s_oh1c[:], s.c_oh1.ap()[:]),
            (s.s_onesall[:], s.c_onesall.ap()[:]), (s.s_istsq[:], s.c_istsq.ap()[:]),
            (s.s_ri64[:], s.c_ri64.ap()[:]), (s.s_ri8[:], s.c_ri8.ap()[:]),
            (s.s_ones1[:], s.c_ones1.ap()[:]), (s.s_g8[:], s.c_g8.ap()[:]),
            (s.s_convb[:], s.d_convb), (s.s_dtb[:], s.d_dtb),
            (s.s_dfold[:], s.d_dfold), (s.s_nw[:], s.d_nw),
            (s.s_gam[:], s.d_gam), (s.s_bet[:], s.d_bet), (s.s_os[:], s.d_os),
            (s.s_kr[:], s.d_knots), (s.s_knc[:], s.d_knotsc),
            (s.s_cm[:], s.c_cm.ap()[:]), (s.s_cb1[:], s.d_cb1), (s.s_cb2[:], s.d_cb2),
            (s.s_b1all[:], s.d_b1all), (s.s_b2os[:], s.d_b2all),
        ]:
            s.dma_sy(dst, src)
        for fc in range(FCC):
            s.dma_sy(s.s_cwcol[:, fc * 4:(fc + 1) * 4], s.d_convw[fc * 128:(fc + 1) * 128, :])
        s.dma_gp(s.s_identb[:], s.c_identb.ap()[:])
        s.dma_gp(s.s_ist[:], s.c_ist.ap()[:])
        s.dma_gp(s.s_idsh[:], s.c_idsh.ap()[:])
        s.dma_sy(s.s_m01[:], s.c_m01.ap()[:])
        s.dma_gp(s.s_negea[:], s.d_alog)
        s.dma_gp(s.s_itemp[:], s.d_temp)
        s.dma_gp(s.s_w1all[:], s.d_w1all)
        s.dma_gp(s.s_dirs[:], s.d_dirs)
        for k in range(KD):
            s.dma_gp(s.s_cw1[:, k * 256:(k + 1) * 256], s.d_cw1[k * 128:(k + 1) * 128, :])
        for k in range(2):
            s.dma_gp(s.s_cw2[:, k * 2:(k + 1) * 2], s.d_cw2[k * 128:(k + 1) * 128, :])
        for k in range(8):
            s.dma_gp(s.s_w2bd[:, k * 64:(k + 1) * 64], s.d_w2bd[k * 128:(k + 1) * 128, :])
        stage = s.s_zfm
        s.wait("gp", "dS")
        for kc in range(16):
            s.dma_gp(stage[:, 0:D], s.d_woutT[kc * 128:(kc + 1) * 128, :])
            s.wait("ve", "dG")
            s.ve(lambda e, kc=kc: e.tensor_scalar_mul(
                out=stage[:, 0:D], in0=stage[:, 0:D], scalar1=s.s_nw[:, kc:kc + 1]))
            s.wait("sy", "ve")
            s.emit_i("sy", "dS", 16, lambda eng, kc=kc: eng.dma_start(
                out=s.d_woutT_bf[kc * 128:(kc + 1) * 128, :], in_=stage[:, 0:D]))
            s.wait("gp", "dS"); s.wait("ve", "dS")
        for k in range(KD):
            for h in range(2):
                cw = DIP // 2
                s.dma_gp(stage[:, 0:cw], s.d_winT[k * 128:(k + 1) * 128, h * cw:(h + 1) * cw])
                s.wait("sy", "dG")
                s.emit_i("sy", "dS", 16, lambda eng, k=k, h=h, cw=cw: eng.dma_start(
                    out=s.d_winT_bf[k * 128:(k + 1) * 128, h * cw:(h + 1) * cw], in_=stage[:, 0:cw]))
                s.wait("gp", "dS")

        s.wait("ve", "dS"); s.wait("ve", "dG")
        s.wait("se", "dS"); s.wait("se", "dG")
        s.se(lambda e: e.activation(out=s.s_negea[:], in_=s.s_negea[:], func=AF.Exp))
        s.wait("ve", "se")
        s.ve(lambda e: e.tensor_scalar_mul(out=s.s_negea[:], in0=s.s_negea[:], scalar1=-1.0))
        s.ve(lambda e: e.tensor_scalar_add(out=s.s_itemp[:], in0=s.s_itemp[:], scalar1=1e-6))
        s.ve(lambda e: e.reciprocal(out=s.s_itemp[:], in_=s.s_itemp[:]))
        s.ve(lambda e: e.tensor_scalar_mul(out=s.s_nitemp[:], in0=s.s_itemp[:], scalar1=-1.0))
        s.ve(lambda e: e.tensor_mul(out=s.s_b2os[:], in0=s.s_b2os[:], in1=s.s_os[:]))
        s.wait("te", "ve"); s.wait("te", "dS")
        s.te(lambda e: e.matmul(out=s.bank(7, 0, 8),
                                lhsT=s.s_knc.broadcast_to([8, 128]), rhs=s.s_cm[:],
                                start=True, stop=True))
        s.te(lambda e: e.matmul(out=s.bank(7, 8, 1), lhsT=s.s_ones1[:], rhs=s.s_kr[:, 0:1], start=True, stop=True))
        s.wait("se", "te")
        s.se(lambda e: e.copy(out=s.s_ccol[:], in_=s.bank(7, 0, 8)))
        s.se(lambda e: e.copy(out=s.s_k0[:], in_=s.bank(7, 8, 1)))
        for fc in range(FCC):
            for w in range(4):
                dst = s.s_convdiag[:, (fc * 4 + w) * 128:(fc * 4 + w + 1) * 128]
                col = s.s_cwcol[:, fc * 4 + w: fc * 4 + w + 1]
                s.ve(lambda e, dst=dst, col=col: e.tensor_tensor(
                    out=dst, in0=s.s_ident[:], in1=col.broadcast_to([128, 128]), op=ALU.mult))
        sigq = s.s_attn
        s.dma_sy(sigq[0:64, 0:D], s.d_sig)
        s.wait("ve", "dS")
        s.ve(lambda e: e.tensor_scalar(out=s.s_scr[0:64, 0:D], in0=sigq[0:64, 0:D],
                                       scalar1=0.3, scalar2=None, op0=ALU.is_gt))
        s.ve(lambda e: e.tensor_scalar(out=s.s_scr2[0:64, 0:D], in0=sigq[0:64, 0:D],
                                       scalar1=-0.3, scalar2=None, op0=ALU.is_lt))
        s.ve(lambda e: e.tensor_sub(out=sigq[0:64, 0:D], in0=s.s_scr[0:64, 0:D], in1=s.s_scr2[0:64, 0:D]))
        s.wait("te", "ve")
        s.te(lambda e: e.matmul(out=s.ps[0:8, 6 * 512:6 * 512 + 512], lhsT=s.s_g8[:],
                                rhs=sigq[0:64, 0:512], start=True, stop=True))
        s.te(lambda e: e.matmul(out=s.ps[0:8, 7 * 512:7 * 512 + 512], lhsT=s.s_g8[:],
                                rhs=sigq[0:64, 512:1024], start=True, stop=True))
        s.wait("se", "te")
        s.se(lambda e: e.sign(out=s.s_scr[0:8, 0:512], in_=s.ps[0:8, 6 * 512:6 * 512 + 512]))
        s.se(lambda e: e.sign(out=s.s_scr[0:8, 512:1024], in_=s.ps[0:8, 7 * 512:7 * 512 + 512]))
        s.wait("gp", "se")
        s.dma_gp(sigq[64:72, 0:D], s.s_scr[0:8, 0:D])
        sigb = s.s_scr.ap().bitcast(BF16)
        s.wait("ve", "se"); s.wait("ve", "dG")
        s.ve(lambda e: e.tensor_copy(out=sigb[0:72, 0:D], in_=sigq[0:72, 0:D]))
        s.wait("te", "ve"); s.wait("te", "dG")
        for k in range(KD):
            s.te(lambda e, k=k: e.transpose(out=s.psb[:, 6 * 1024:6 * 1024 + 72],
                                            in_=sigb[0:72, k * 128:(k + 1) * 128],
                                            identity=s.s_identb[0:72, 0:72]))
            s.wait("se", "te")
            s.se(lambda e, k=k: e.copy(out=s.s_sig72[:, k * 72:(k + 1) * 72],
                                       in_=s.psb[:, 6 * 1024:6 * 1024 + 72]))
            s.wait("te", "se")
        s.ve(lambda e: e.memset(s.s_xbctail[:], 0.0))
        s.ve(lambda e: e.memset(s.s_prev[:], 0.0))
        s.ve(lambda e: e.memset(s.s_prevb[:], 0.0))
        s.barrier()

    def fetch_u(self, tile, buf):
        for q in range(2):
            self.dma_sy(self.s_x[buf][:, q * D:(q + 1) * D],
                        self.d_u[tile * TT + q * 128: tile * TT + (q + 1) * 128, :])

    def fetch_win(self, grp, buf):
        mlo, mhi = grp
        ncols = min(mhi * 128, DIP) - mlo * 128
        for k in range(KD):
            dma = self.dma_sy if k % 2 == 0 else self.dma_gp
            dma(self.s_win[buf][:, (k * 3) * 128:(k * 3) * 128 + ncols],
                self.d_winT_bf[k * 128:(k + 1) * 128, mlo * 128:mlo * 128 + ncols])
        return {"dS": self.cnt["dS"], "dG": self.cnt["dG"]}

    def ln_tm(self, src, dst, slot):
        s = self
        s.wait("se", "ve")
        s.se(lambda e: e.activation(out=s.s_scr2[:, 0:D], in_=src, func=AF.Identity,
                                    accum_out=s.s_m1[:, slot:slot + 1]))
        s.wait("ve", "se")
        s.ve(lambda e: e.tensor_scalar_mul(out=s.s_mean[:, slot:slot + 1],
                                           in0=s.s_m1[:, slot:slot + 1], scalar1=1.0 / D))
        s.wait("ve", "ve")
        s.ve(lambda e: e.tensor_scalar(out=s.s_scr[:, 0:D], in0=src,
                                       scalar1=s.s_mean[:, slot:slot + 1], scalar2=None,
                                       op0=ALU.subtract))
        if DEBUG_TAPS and slot < 2:
            s.ve(lambda e, slot=slot: e.tensor_copy(out=s.s_csnap[:, slot * D:(slot + 1) * D],
                                                    in_=s.s_scr[:, 0:D]))
        s.wait("se", "ve")
        s.se(lambda e: e.activation(out=s.s_scr2[:, 0:D], in_=s.s_scr[:, 0:D], func=AF.Square,
                                    accum_out=s.s_m2[:, slot:slot + 1]))
        s.wait("ve", "se")
        s.ve(lambda e: e.tensor_scalar_mul(out=s.s_m1[:, slot:slot + 1],
                                           in0=s.s_m2[:, slot:slot + 1], scalar1=1.0 / D))
        s.wait("se", "ve")
        s.se(lambda e: e.activation(out=s.s_rstd[:, slot:slot + 1], in_=s.s_m1[:, slot:slot + 1],
                                    func=AF.Ln, bias=EPS))
        s.se(lambda e: e.activation(out=s.s_rstd[:, slot:slot + 1], in_=s.s_rstd[:, slot:slot + 1],
                                    func=AF.Exp, scale=-0.5))
        s.wait("ve", "se")
        s.ve(lambda e: e.tensor_scalar_mul(out=dst, in0=s.s_scr[:, 0:D],
                                           scalar1=s.s_rstd[:, slot:slot + 1]))

    def transpose_fm(self, src_b16_2d, dst_fm, nk, tok_off, ncols=128, identity=None):
        s = self
        ident = identity if identity is not None else s.s_identb[:]
        for k in range(nk):
            b = k % 2
            s.te(lambda e, k=k, b=b: e.transpose(out=s.bank(b, 0, ncols), in_=src_b16_2d(k),
                                                 identity=ident))
            s.wait("se", "te")
            s.se(lambda e, k=k, b=b: e.copy(out=dst_fm(k), in_=s.bank(b, 0, ncols)))
            s.wait("te", "se", s.cnt["se"] - 1)

    def tile_ln1(self, tile, xb, is_ln3=False):
        s = self
        for q in range(2):
            s.ln_tm(xb[:, q * D:(q + 1) * D], s.s_ln[:, q * D:(q + 1) * D], q)
        s.wait("te", "ve")
        for q in range(2):
            for k in range(KD):
                b = k % 2
                s.te(lambda e, q=q, k=k, b=b: e.transpose(
                    out=s.psb[:, b * 1024:b * 1024 + 128],
                    in_=s.s_ln[:, q * D + k * 128: q * D + (k + 1) * 128],
                    identity=s.s_identb[:]))
                s.wait("se", "te")
                s.se(lambda e, q=q, k=k, b=b: e.copy(
                    out=s.s_lnfm[:, k * TT + q * 128: k * TT + q * 128 + 128],
                    in_=s.psb[:, b * 1024:b * 1024 + 128]))
                s.wait("te", "se", s.cnt["se"] - 1)

    def tile_inproj(self, tile):
        s = self
        groups = WIN_GROUPS
        s.wait("te", "se")
        for g, (mlo, mhi) in enumerate(groups):
            buf = g % 3
            for kk, v in s.win_marks[g].items():
                s.wait("te", kk, v)
            nm = mhi - mlo
            for mi in range(nm):
                m = mlo + mi
                b = m % 2
                rows = 128 if m < 34 else 32
                if len(s.evac_hist) >= 2:
                    ek, ev = s.evac_hist[-2]
                    s.wait("te", ek, ev)
                for k in range(KD):
                    lhsT = s.s_win[buf][:, (k * 3 + mi) * 128:(k * 3 + mi) * 128 + rows]
                    rhs = s.s_lnfm[:, k * TT:(k + 1) * TT]
                    s.te(lambda e, lhsT=lhsT, rhs=rhs, k=k, b=b, rows=rows: e.matmul(
                        out=s.ps[0:rows, b * 512:b * 512 + TT], lhsT=lhsT, rhs=rhs,
                        start=(k == 0), stop=(k == KD - 1)))
                te_mark = s.cnt["te"]
                if m < 16:
                    dst = s.s_zfm[:, m * TT:(m + 1) * TT]
                    s.wait("se", "te", te_mark)
                    ev = s.se(lambda e, dst=dst, b=b: e.activation(
                        out=dst, in_=s.ps[:, b * 512:b * 512 + TT], func=AF.Silu))
                    s.evac_hist.append(("se", ev))
                elif m < 34:
                    fc = m - 16
                    dst = s.s_xbc[:, fc * (TT + 3) + 3: fc * (TT + 3) + 3 + TT]
                    s.wait("ve", "te", te_mark)
                    ev = s.ve(lambda e, dst=dst, b=b: e.tensor_copy(
                        out=dst, in_=s.ps[:, b * 512:b * 512 + TT]))
                    s.evac_hist.append(("ve", ev))
                else:
                    s.wait("ve", "te", te_mark)
                    ev = s.ve(lambda e, b=b: e.tensor_copy(
                        out=s.s_dtraw[:, :], in_=s.ps[0:H, b * 512:b * 512 + TT]))
                    s.evac_hist.append(("ve", ev))
            te_mark = s.cnt["te"]
            if g + 2 < NGROUPS_WIN:
                s.wait("sy", "te", te_mark)
                s.wait("gp", "te", te_mark)
                s.win_marks[g + 2] = s.fetch_win(groups[g + 2], (g + 2) % 3)
            elif tile + 1 < NTILE and (g + 2 - NGROUPS_WIN) < 2:
                gg = g + 2 - NGROUPS_WIN
                s.wait("sy", "te", te_mark)
                s.wait("gp", "te", te_mark)
                s.next_win_marks[gg] = s.fetch_win(groups[gg], gg % 3)

    def tile_conv(self, tile):
        s = self
        if tile == 0:
            s.wait("gp", "ve")
            for fc in range(FCC):
                s.dma_gp(s.s_xbc[:, fc * (TT + 3): fc * (TT + 3) + 3],
                         s.s_xbctail[:, fc * 3:(fc + 1) * 3])
            s.tail_mark = s.cnt["dG"]
        s.wait("te", "dG", s.tail_mark)
        s.wait("te", "se")
        s.wait("te", "ve")
        for fc in range(FCC):
            b = fc % 2
            if len(s.evac_hist) >= 2:
                ek, ev = s.evac_hist[-2]
                s.wait("te", ek, ev)
            for w in range(4):
                lhsT = s.s_convdiag[:, (fc * 4 + w) * 128:(fc * 4 + w + 1) * 128]
                rhs = s.s_xbc[:, fc * (TT + 3) + w: fc * (TT + 3) + w + TT]
                s.te(lambda e, lhsT=lhsT, rhs=rhs, b=b, w=w: e.matmul(
                    out=s.ps[:, b * 512:b * 512 + TT], lhsT=lhsT, rhs=rhs,
                    start=(w == 0), stop=(w == 3)))
            s.wait("se", "te")
            bias = s.s_convb[:, fc:fc + 1]
            if fc < FCX:
                dst = s.s_xfm[:, fc * TT:(fc + 1) * TT]
            elif fc == 16:
                dst = s.s_bfm[:, 0:TT]
            else:
                dst = s.s_cfm[:, 0:TT]
            ev = s.se(lambda e, dst=dst, b=b, bias=bias: e.activation(
                out=dst, in_=s.ps[:, b * 512:b * 512 + TT], func=AF.Silu, bias=bias))
            s.evac_hist.append(("se", ev))
        if tile + 1 < NTILE:
            s.wait("gp", "se")
            s.wait("gp", "ve")
            for fc in range(FCC):
                s.dma_gp(s.s_xbctail[:, fc * 3:(fc + 1) * 3],
                         s.s_xbc[:, fc * (TT + 3) + TT: fc * (TT + 3) + TT + 3])
            s.wait("gp", "te")
            for fc in range(FCC):
                s.dma_gp(s.s_xbc[:, fc * (TT + 3): fc * (TT + 3) + 3],
                         s.s_xbctail[:, fc * 3:(fc + 1) * 3])
            s.tail_mark = s.cnt["dG"]

    def tile_dt(self, tile):
        s = self
        s.wait("se", "ve")
        s.se(lambda e: e.activation(out=s.s_dtraw[:], in_=s.s_dtraw[:], func=AF.Identity, bias=s.s_dtb[:]))
        s.se(lambda e: e.activation(out=s.s_sp1[:], in_=s.s_dtraw[:], func=AF.Abs))
        s.se(lambda e: e.activation(out=s.s_sp1[:], in_=s.s_sp1[:], func=AF.Exp, scale=-1.0))
        s.se(lambda e: e.activation(out=s.s_sp1[:], in_=s.s_sp1[:], func=AF.Ln, bias=1.0))
        s.se(lambda e: e.activation(out=s.s_dtfm[:], in_=s.s_dtraw[:], func=AF.Relu))
        s.wait("ve", "se")
        s.ve(lambda e: e.tensor_add(out=s.s_dtfm[:], in0=s.s_dtfm[:], in1=s.s_sp1[:]))

    def ssm_chunk(self, tile, c):
        s = self
        B6 = lambda lo, n, p0=0, np_=128: s.bank(6, lo, n, p0, np_)
        B7 = lambda lo, n, p0=0, np_=128: s.bank(7, lo, n, p0, np_)

        s.wait("te", "ve")
        s.te(lambda e: e.transpose(out=B6(0, H, 0, 64), in_=s.s_dtfm[0:H, c * 64:(c + 1) * 64],
                                   identity=s.s_ident[0:H, 0:H]))
        s.wait("se", "te")
        s.se(lambda e: e.copy(out=s.s_dt0[:], in_=B6(0, H, 0, 64)))
        s.se(lambda e: e.activation(out=s.s_lndt0[:], in_=s.s_dt0[:], func=AF.Ln))
        s.wait("ve", "se")
        s.ve(lambda e: e.tensor_tensor(out=s.s_da0[:], in0=s.s_dt0[:],
                                       in1=s.s_negea[0:64, :], op=ALU.mult))
        s.wait("te", "ve")
        s.te(lambda e: e.matmul(out=B6(32, H, 0, 64), lhsT=s.s_lt[:], rhs=s.s_da0[:],
                                start=True, stop=True))
        s.wait("se", "te")
        s.se(lambda e: e.copy(out=s.s_ca0[:], in_=B6(32, H, 0, 64)))
        s.wait("ve", "se")
        s.ve(lambda e: e.tensor_sub(out=s.s_d10[:], in0=s.s_ca0[:], in1=s.s_lndt0[:]))
        s.wait("te", "ve")
        s.te(lambda e: e.matmul(out=B6(64, 16), lhsT=s.s_ih0[:], rhs=s.s_d10[:, 0:16],
                                start=True, stop=False))
        s.te(lambda e: e.matmul(out=B6(64, 16), lhsT=s.s_ih1[:], rhs=s.s_d10[:, 16:32],
                                start=False, stop=True))
        s.te(lambda e: e.matmul(out=B6(80, 16), lhsT=s.s_oh0c[:], rhs=s.s_da0[:, 0:16],
                                start=True, stop=False))
        s.te(lambda e: e.matmul(out=B6(80, 16), lhsT=s.s_oh1c[:], rhs=s.s_da0[:, 16:32],
                                start=False, stop=False))
        s.te(lambda e: e.matmul(out=B6(80, 16), lhsT=s.s_nih0[:], rhs=s.s_ca0[:, 0:16],
                                start=False, stop=False))
        s.te(lambda e: e.matmul(out=B6(80, 16), lhsT=s.s_nih1[:], rhs=s.s_ca0[:, 16:32],
                                start=False, stop=True))
        s.te(lambda e: e.matmul(out=B6(96, 16), lhsT=s.s_ih0[:], rhs=s.s_dt0[:, 0:16],
                                start=True, stop=False))
        s.te(lambda e: e.matmul(out=B6(96, 16), lhsT=s.s_ih1[:], rhs=s.s_dt0[:, 16:32],
                                start=False, stop=True))
        s.te(lambda e: e.matmul(out=B6(112, 16), lhsT=s.s_ih0[:], rhs=s.s_ca0[:, 0:16],
                                start=True, stop=False))
        s.te(lambda e: e.matmul(out=B6(112, 16), lhsT=s.s_ih1[:], rhs=s.s_ca0[:, 16:32],
                                start=False, stop=True))
        s.te(lambda e: e.matmul(out=B6(128, H), lhsT=s.s_onesall[:], rhs=s.s_da0[:],
                                start=True, stop=True))
        s.wait("ve", "se")
        s.ve(lambda e: e.tensor_copy(out=s.s_b2[:],
                                     in_=s.s_bfm[:, None, c * 64:(c + 1) * 64].broadcast_to([128, 2, 64])))
        s.wait("te", "ve")
        s.te(lambda e: e.matmul(out=B7(0, 64), lhsT=s.s_b2[:],
                                rhs=s.s_cfm[:, c * 64:(c + 1) * 64], start=True, stop=True))
        s.wait("se", "te")
        s.se(lambda e: e.copy(out=s.s_blob[:], in_=B6(64, 16)))
        s.se(lambda e: e.activation(out=s.s_texp[:], in_=B6(80, 16), func=AF.Exp))
        s.se(lambda e: e.activation(out=s.s_e3f[:], in_=B6(112, 16), func=AF.Exp))
        s.se(lambda e: e.activation(out=s.s_cdec[:], in_=B6(128, H), func=AF.Exp))
        s.se(lambda e: e.copy(out=s.s_cbt2[:], in_=B7(0, 64)))
        s.wait("ve", "se")
        s.ve(lambda e: e.tensor_mul(out=s.s_cbt2[:], in0=s.s_cbt2[:], in1=s.s_m01[:]))
        s.wait("ve", "se")
        s.ve(lambda e: e.tensor_tensor(out=s.s_e2f[:], in0=s.s_texp[:], in1=B6(96, 16), op=ALU.mult))
        s.wait("ve", "ve")

        s.wait("te", "ve")
        for j in range(16):
            for hh in range(2):
                col = s.s_ca0[:, j + 16 * hh: j + 16 * hh + 1].broadcast_to([64, 64])
                if hh == 0:
                    s.te(lambda e, j=j, col=col: e.transpose(
                        out=s.ps[0:64, 1024 + j * 64:1024 + (j + 1) * 64],
                        in_=col, identity=s.s_ident[0:64, 0:64]))
                else:
                    s.te(lambda e, j=j, col=col: e.matmul(
                        out=s.ps[64:128, 1024 + j * 64:1024 + (j + 1) * 64],
                        lhsT=col, rhs=s.s_ident[0:64, 0:64],
                        start=True, stop=True, tile_position=(0, 64)))

        s.wait("ve", "te")
        s.ve(lambda e: e.tensor_tensor(out=s.s_diff[:], in0=s.ps[:, 1024:2048],
                                       in1=s.s_blob[:, :, None].broadcast_to([128, 16, 64]),
                                       op=ALU.subtract))
        s.ve(lambda e: e.tensor_scalar(out=s.s_diff[:], in0=s.s_diff[:],
                                       scalar1=30.0, scalar2=None, op0=ALU.min))
        s.wait("se", "ve")
        s.se(lambda e: e.activation(out=s.s_diff[:], in_=s.s_diff[:], func=AF.Exp))
        s.wait("ve", "se")
        s.ve(lambda e: e.tensor_tensor(out=s.s_mblob[:], in0=s.s_diff[:],
                                       in1=s.s_cbt2[:, None, :].broadcast_to([128, 16, 64]),
                                       op=ALU.mult))

        s.wait("te", "ve")
        for fc in range(FCX):
            if fc < 8:
                s.te(lambda e, fc=fc: e.matmul(
                    out=s.ps[0:64, 2048 + fc * 128:2048 + (fc + 1) * 128],
                    lhsT=s.s_xfm[:, fc * TT + c * 64: fc * TT + (c + 1) * 64],
                    rhs=s.s_identb[:], start=True, stop=True))
            else:
                s.te(lambda e, fc=fc: e.matmul(
                    out=s.ps[64:128, 2048 + (fc - 8) * 128:2048 + (fc - 7) * 128],
                    lhsT=s.s_xfm[:, fc * TT + c * 64: fc * TT + (c + 1) * 64],
                    rhs=s.s_identb[:], start=True, stop=True, tile_position=(0, 64)))
        s.wait("se", "te")
        s.se(lambda e: e.copy(out=s.s_xtm[:], in_=s.ps[:, 2048:3072]))
        s.wait("te", "se")
        for fc in range(FCX):
            if fc < 8:
                s.te(lambda e, fc=fc: e.matmul(
                    out=s.ps[0:64, 2048 + fc * 128:2048 + (fc + 1) * 128],
                    lhsT=s.s_zfm[:, fc * TT + c * 64: fc * TT + (c + 1) * 64],
                    rhs=s.s_identb[:], start=True, stop=True))
            else:
                s.te(lambda e, fc=fc: e.matmul(
                    out=s.ps[64:128, 2048 + (fc - 8) * 128:2048 + (fc - 7) * 128],
                    lhsT=s.s_zfm[:, fc * TT + c * 64: fc * TT + (c + 1) * 64],
                    rhs=s.s_identb[:], start=True, stop=True, tile_position=(0, 64)))
        s.te(lambda e: e.matmul(out=s.ps[0:64, 3072:3200],
                                lhsT=s.s_bfm[:, c * 64:(c + 1) * 64], rhs=s.s_identb[:],
                                start=True, stop=True))
        s.te(lambda e: e.matmul(out=s.ps[64:128, 3072:3200],
                                lhsT=s.s_bfm[:, c * 64:(c + 1) * 64], rhs=s.s_identb[:],
                                start=True, stop=True, tile_position=(0, 64)))
        s.wait("se", "te")
        s.se(lambda e: e.copy(out=s.s_zc[:], in_=s.ps[:, 2048:3072]))
        s.se(lambda e: e.copy(out=s.s_btm[:], in_=s.ps[:, 3072:3200]))
        s.wait("ve", "se")
        s.ve(lambda e: e.tensor_tensor(out=s.s_xs[:], in0=s.s_xtm[:],
                                       in1=s.s_e2f[:, :, None].broadcast_to([128, NH, 64]),
                                       op=ALU.mult))
        s.ve(lambda e: e.tensor_tensor(out=s.s_xd[:], in0=s.s_xtm[:],
                                       in1=s.s_dfold[:, :, None].broadcast_to([128, NH, 64]),
                                       op=ALU.mult))

        for half in range(2):
            p0 = half * 64
            tp = None if half == 0 else (0, 64)
            for dm in range(2):
                s.te(lambda e, half=half, dm=dm, p0=p0: e.matmul(
                    out=s.ps[p0:p0 + 64, 1024 + dm * 512:1024 + (dm + 1) * 512],
                    lhsT=s.s_cfm[:, c * 64:(c + 1) * 64],
                    rhs=s.s_prevb[:, half * 1024 + dm * 512: half * 1024 + (dm + 1) * 512],
                    start=True, stop=True,
                    tile_position=(0, p0)))
            s.wait("ve", "te")
            s.ve(lambda e, p0=p0: e.tensor_tensor(
                out=s.ps[p0:p0 + 64, 1024:2048], in0=s.ps[p0:p0 + 64, 1024:2048],
                in1=s.s_e3f[p0:p0 + 64, :, None].broadcast_to([64, NH, 64]), op=ALU.mult))
            s.wait("te", "ve")
            for j in range(NH):
                s.te(lambda e, half=half, j=j, p0=p0: e.matmul(
                    out=s.ps[p0:p0 + 64, 1024 + j * 64:1024 + (j + 1) * 64],
                    lhsT=s.s_mblob[p0:p0 + 64, j * 64:(j + 1) * 64],
                    rhs=s.s_xtm[p0:p0 + 64, j * 64:(j + 1) * 64],
                    start=False, stop=True,
                    tile_position=(p0, p0)))
            s.wait("ve", "te")
            s.ve(lambda e, p0=p0: e.tensor_tensor(
                out=s.s_yc[p0:p0 + 64, :], in0=s.ps[p0:p0 + 64, 1024:2048],
                in1=s.s_xd[p0:p0 + 64, :], op=ALU.add))

        s.wait("te", "ve")
        for half in range(2):
            p0 = half * 64
            for dm in range(2):
                s.te(lambda e, half=half, dm=dm, p0=p0: e.matmul(
                    out=s.ps[:, 2048 + half * 1024 + dm * 512: 2048 + half * 1024 + (dm + 1) * 512],
                    lhsT=s.s_btm[p0:p0 + 64, :],
                    rhs=s.s_xs[p0:p0 + 64, dm * 512:(dm + 1) * 512],
                    start=True, stop=True))
        s.wait("ve", "te")
        for half in range(2):
            s.ve(lambda e, half=half: e.tensor_tensor(
                out=s.s_stt[:], in0=s.s_prev[:, half * 1024:(half + 1) * 1024],
                in1=s.s_cdec[:, half * 16:(half + 1) * 16, None].broadcast_to([128, 16, 64]),
                op=ALU.mult))
            s.ve(lambda e, half=half: e.tensor_add(
                out=s.s_prev[:, half * 1024:(half + 1) * 1024], in0=s.s_stt[:],
                in1=s.ps[:, 2048 + half * 1024:2048 + (half + 1) * 1024]))
        s.wait("se", "ve")
        s.se(lambda e: e.copy(out=s.s_prevb[:], in_=s.s_prev[:]))

        s.ve(lambda e: e.tensor_mul(out=s.s_yc[:], in0=s.s_yc[:], in1=s.s_zc[:]))
        s.wait("se", "ve")
        s.se(lambda e: e.activation(out=s.s_scr[0:64, :], in_=s.s_yc[0:64, :],
                                    func=AF.Square, accum_out=s.s_ss[0:64, :]))
        s.se(lambda e: e.activation(out=s.s_scr[64:128, :], in_=s.s_yc[64:128, :],
                                    func=AF.Square, accum_out=s.s_ss[64:128, :]))
        s.wait("ve", "se")
        s.ve(lambda e: e.tensor_copy(out=s.s_rs[:], in_=s.s_ss[:]))
        s.wait("te", "ve")
        s.te(lambda e: e.matmul(out=s.bank(6, 160, 1), lhsT=s.s_istsq[:], rhs=s.s_rs[:],
                                start=True, stop=True))
        s.wait("se", "te")
        s.se(lambda e: e.activation(out=s.s_rs[:], in_=s.bank(6, 160, 1), func=AF.Ln,
                                    scale=1.0 / DI, bias=EPS))
        s.se(lambda e: e.activation(out=s.s_rs[:], in_=s.s_rs[:], func=AF.Exp, scale=-0.5))
        s.wait("ve", "se")
        s.ve(lambda e: e.tensor_scalar_mul(out=s.s_yn[:], in0=s.s_yc[:], scalar1=s.s_rs[:]))
        s.wait("te", "ve")
        for half in range(2):
            p0 = half * 64
            for cs in range(8):
                if half == 0:
                    s.te(lambda e, cs=cs: e.matmul(
                        out=s.ps[:, cs * 64:(cs + 1) * 64],
                        lhsT=s.s_yn[0:64, cs * 128:(cs + 1) * 128],
                        rhs=s.s_identb[0:64, 0:64], start=True, stop=True))
                else:
                    s.te(lambda e, cs=cs: e.matmul(
                        out=s.ps[:, 512 + cs * 64:512 + (cs + 1) * 64],
                        lhsT=s.s_yn[64:128, cs * 128:(cs + 1) * 128],
                        rhs=s.s_idsh[64:128, 0:64], start=True, stop=True))
            s.wait("se", "te")
            s.se(lambda e, half=half, c=c: e.copy(
                out=s.s_yfm[:, :].rearrange("p (fc t) -> p fc t", fc=16)[:, half * 8:(half + 1) * 8, c * 64:(c + 1) * 64],
                in_=s.ps[:, half * 512:(half + 1) * 512].rearrange("p (fc t) -> p fc t", fc=8)))
            s.wait("te", "se")

    def fetch_wout(self, mc, buf):
        self.dma_sy(
            self.s_wo[buf][:, :].rearrange("p (kc m) -> p kc m", kc=16),
            self.d_woutT_bf[:, mc * 128:(mc + 1) * 128].rearrange("(kc p) m -> p kc m", p=128))

    def tile_outproj(self, tile, xb):
        s = self
        s.wait("te", "se")
        wmarks = {}
        self.fetch_wout(0, 0); wmarks[0] = s.cnt["dS"]
        self.fetch_wout(1, 1); wmarks[1] = s.cnt["dS"]
        for mc in range(8):
            b = mc % 2
            s.wait("te", "dS", wmarks[mc])
            for kc in range(16):
                s.te(lambda e, mc=mc, kc=kc, b=b: e.matmul(
                    out=s.ps[:, b * 512:b * 512 + TT],
                    lhsT=s.s_wo[mc % 2][:, kc * 128:(kc + 1) * 128],
                    rhs=s.s_yfm[:, kc * TT:(kc + 1) * TT],
                    start=(kc == 0), stop=(kc == 15)))
            if mc + 2 < 8:
                s.wait("sy", "te")
                self.fetch_wout(mc + 2, (mc + 2) % 2)
                wmarks[mc + 2] = s.cnt["dS"]
            s.wait("se", "te")
            buf = s.s_diff if mc < 4 else s.s_yc
            off = (mc % 4) * TT
            s.se(lambda e, b=b, buf=buf, off=off: e.copy(
                out=buf[:, off:off + TT], in_=s.ps[:, b * 512:b * 512 + TT]))
            s.wait("te", "se", s.cnt["se"] - 1)
        for q in range(2):
            for mc in range(8):
                buf = s.s_diff if mc < 4 else s.s_yc
                src = buf[:, (mc % 4) * TT + q * 128:(mc % 4) * TT + q * 128 + 128]
                b = mc % 2
                s.te(lambda e, src=src, b=b: e.transpose(
                    out=s.bank(b, 0, 128), in_=src, identity=s.s_ident[:]))
                s.wait("se", "te")
                s.se(lambda e, q=q, mc=mc, b=b: e.copy(
                    out=s.s_attn[:, q * D + mc * 128: q * D + (mc + 1) * 128],
                    in_=s.bank(b, 0, 128)))
                s.wait("te", "se", s.cnt["se"] - 1)
        for q in range(2):
            s.ln_tm(s.s_attn[:, q * D:(q + 1) * D], s.s_diff[:, 0:D], 2)
            s.ve(lambda e, q=q: e.tensor_add(out=xb[:, q * D:(q + 1) * D],
                                             in0=xb[:, q * D:(q + 1) * D], in1=s.s_diff[:, 0:D]))

    def tile_ffn(self, tile, xb):
        _tile = tile
        s = self
        self.tile_ln1(tile, xb, is_ln3=True)
        s.wait("te", "se")
        for mh in range(2):
            b = mh % 2
            for k in range(KD):
                s.te(lambda e, mh=mh, k=k, b=b: e.matmul(
                    out=s.ps[:, b * 512:b * 512 + TT],
                    lhsT=s.s_cw1[:, k * 256 + mh * 128: k * 256 + (mh + 1) * 128],
                    rhs=s.s_lnfm[:, k * TT:(k + 1) * TT],
                    start=(k == 0), stop=(k == KD - 1)))
            s.wait("se", "te")
            s.se(lambda e, mh=mh, b=b: e.activation(
                out=s.s_hfm[:, mh * TT:(mh + 1) * TT], in_=s.ps[:, b * 512:b * 512 + TT],
                func=AF.Gelu, bias=s.s_cb1[:, mh:mh + 1]))
            s.wait("te", "se", s.cnt["se"] - 1)
        for k in range(2):
            s.te(lambda e, k=k: e.matmul(out=s.bank(6, 0, TT, 0, 2),
                                         lhsT=s.s_cw2[:, k * 2:(k + 1) * 2],
                                         rhs=s.s_hfm[:, k * TT:(k + 1) * TT],
                                         start=(k == 0), stop=(k == 1)))
        s.wait("se", "te")
        s.se(lambda e: e.activation(out=s.s_comp[:], in_=s.bank(6, 0, TT, 0, 2),
                                    func=AF.Tanh, bias=s.s_cb2[:]))
        for q in range(2):
            for k in range(KD):
                s.te(lambda e, q=q, k=k: e.matmul(
                    out=s.bank(7, 0, 72),
                    lhsT=s.s_lnfm[:, k * TT + q * 128: k * TT + q * 128 + 128],
                    rhs=s.s_sig72[:, k * 72:(k + 1) * 72],
                    start=(k == 0), stop=(k == KD - 1)))
            s.wait("se", "te")
            s.se(lambda e, q=q: e.copy(out=s.s_scores[:, q * 72:(q + 1) * 72], in_=s.bank(7, 0, 72)))
            s.wait("te", "se")
        for kg in range(8):
            b = kg % 2
            s.te(lambda e, kg=kg, b=b: e.matmul(
                out=s.ps[:, b * 512:b * 512 + TT],
                lhsT=s.s_w1all[:, kg * 128:(kg + 1) * 128],
                rhs=s.s_comp[:], start=True, stop=True))
            s.wait("se", "te")
            s.se(lambda e, kg=kg, b=b: e.activation(
                out=s.s_hh[:, kg * TT:(kg + 1) * TT], in_=s.ps[:, b * 512:b * 512 + TT],
                func=AF.Relu, bias=s.s_b1all[:, kg:kg + 1]))
            s.wait("te", "se", s.cnt["se"] - 1)
        for kg in range(8):
            s.te(lambda e, kg=kg: e.matmul(
                out=s.bank(6, 0, TT, 0, 64),
                lhsT=s.s_w2bd[:, kg * 64:(kg + 1) * 64],
                rhs=s.s_hh[:, kg * TT:(kg + 1) * TT],
                start=(kg == 0), stop=(kg == 7)))
        s.wait("se", "te")
        s.se(lambda e: e.activation(out=s.s_mag[:], in_=s.bank(6, 0, TT, 0, 64),
                                    func=AF.Identity, bias=s.s_b2os[:], scale=s.s_os[:]))
        for q in range(2):
            sc = s.s_scores[:, q * 72:(q + 1) * 72]
            W = lambda: s.wait("ve", "ve")
            s.se(lambda e, sc=sc: e.activation(out=s.s_rt[:], in_=sc, func=AF.Exp,
                                               scale=s.s_nitemp[:]))
            s.wait("ve", "se")
            s.ve(lambda e: e.tensor_scalar_add(out=s.s_rt2[:], in0=s.s_rt[:], scalar1=1.0)); W()
            s.ve(lambda e: e.reciprocal(out=s.s_rt[:], in_=s.s_rt2[:])); W()
            s.ve(lambda e: e.tensor_scalar_mul(out=s.s_rt2[:], in0=s.s_rt[:], scalar1=7.0)); W()
            gbuf = [s.s_gates, s.s_gates2]
            for k in range(7):
                s.wait("se", "ve")
                s.se(lambda e, k=k: e.activation(out=s.s_rt[:], in_=s.s_rt2[:],
                                                 func=AF.Relu, bias=float(-k)))
                s.wait("ve", "se")
                dst = gbuf[k % 2]
                prv = gbuf[1 - k % 2]
                s.ve(lambda e, k=k: e.tensor_scalar_mul(out=s.s_rt[:], in0=s.s_rt[:],
                                                        scalar1=s.s_ccol[:, k:k + 1])); W()
                if k == 0:
                    s.ve(lambda e, dst=dst: e.tensor_add(
                        out=dst[:], in0=s.s_rt[:], in1=s.s_k0.broadcast_to([128, 72])))
                else:
                    s.ve(lambda e, dst=dst, prv=prv: e.tensor_add(
                        out=dst[:], in0=s.s_rt[:], in1=prv[:]))
                W()
            gfin = gbuf[6 % 2]
            W = lambda: s.wait("ve", "ve")
            s.ve(lambda e, gfin=gfin: e.max(out=s.s_rt[:, 0:8], in_=gfin[:, 64:72])); W()
            if q == 0 and _tile == TAP_TILE:
                s.tap("mx1k", s.s_rt[:, 0:8], [128, 8])
                s.tap("cgk", gfin[:, 64:72], [128, 8])
            s.ve(lambda e, gfin=gfin: e.tensor_scalar(out=s.s_rt[:, 8:16], in0=gfin[:, 64:72],
                                           scalar1=s.s_rt[:, 0:1], scalar2=None, op0=ALU.is_ge)); W()
            if q == 0 and _tile == TAP_TILE:
                s.tap("ismk", s.s_rt[:, 8:16], [128, 8])
            s.ve(lambda e: e.tensor_mul(out=s.s_rt[:, 16:24], in0=s.s_rt[:, 8:16], in1=s.s_ri8[:])); W()
            if q == 0 and _tile == TAP_TILE:
                s.tap("valk", s.s_rt[:, 16:24], [128, 8])
            s.ve(lambda e: e.max(out=s.s_rt[:, 0:8], in_=s.s_rt[:, 16:24])); W()
            s.ve(lambda e: e.tensor_scalar(out=s.s_rt[:, 8:16], in0=s.s_rt[:, 16:24],
                                           scalar1=s.s_rt[:, 0:1], scalar2=None, op0=ALU.is_ge)); W()
            if q == 0 and _tile == TAP_TILE:
                s.tap("ohck", s.s_rt[:, 8:16], [128, 8])
            ohx = s.s_rt[:, 8:16, None].broadcast_to([128, 8, 8])
            s.ve(lambda e, gfin=gfin, ohx=ohx: e.tensor_tensor(out=s.s_rt2[:, 0:64], in0=gfin[:, 0:64],
                                           in1=ohx, op=ALU.mult))
            s.ve(lambda e, ohx=ohx: e.tensor_scalar(
                out=s.s_gates2[:, 0:64], in0=ohx,
                scalar1=1.0, scalar2=None, op0=ALU.subtract)); W()
            s.ve(lambda e: e.tensor_add(out=s.s_gates2[:, 0:64], in0=s.s_gates2[:, 0:64],
                                        in1=s.s_rt2[:, 0:64])); W()
            s.ve(lambda e: e.max(out=s.s_rt[:, 0:8], in_=s.s_gates2[:, 0:64])); W()
            s.ve(lambda e: e.tensor_scalar(out=s.s_rt2[:, 0:64], in0=s.s_gates2[:, 0:64],
                                           scalar1=s.s_rt[:, 0:1], scalar2=None, op0=ALU.is_ge)); W()
            s.ve(lambda e: e.tensor_mul(out=s.s_rt2[:, 0:64], in0=s.s_rt2[:, 0:64], in1=s.s_ri64[:])); W()
            s.ve(lambda e: e.max(out=s.s_rt[:, 0:8], in_=s.s_rt2[:, 0:64])); W()
            s.ve(lambda e, q=q: e.tensor_scalar(out=s.s_oh[:, q * 64:(q + 1) * 64],
                                                in0=s.s_rt2[:, 0:64],
                                                scalar1=s.s_rt[:, 0:1], scalar2=None, op0=ALU.is_ge)); W()
        s.wait("te", "ve")
        for q in range(2):
            s.te(lambda e, q=q: e.transpose(out=s.psb[0:64, 7 * 1024 + 256:7 * 1024 + 384],
                                            in_=s.s_oh[:, q * 64:(q + 1) * 64],
                                            identity=s.s_identb[:]))
            s.wait("ve", "te")
            s.ve(lambda e, q=q: e.tensor_tensor(out=s.s_ohfm[:, q * 128:(q + 1) * 128],
                                                in0=s.psb[0:64, 7 * 1024 + 256:7 * 1024 + 384],
                                                in1=s.s_mag[:, q * 128:(q + 1) * 128], op=ALU.mult))
            s.wait("te", "ve")
        for q in range(2):
            for dm in range(2):
                b = dm
                s.te(lambda e, q=q, dm=dm, b=b: e.matmul(
                    out=s.ps[:, b * 512:(b) * 512 + 512],
                    lhsT=s.s_ohfm[:, q * 128:(q + 1) * 128],
                    rhs=s.s_dirs[:, dm * 512:(dm + 1) * 512],
                    start=True, stop=True))
            s.wait("ve", "te")
            for dm in range(2):
                s.ve(lambda e, q=q, dm=dm: e.tensor_add(
                    out=s.s_attn[:, q * D + dm * 512: q * D + (dm + 1) * 512],
                    in0=s.ps[:, dm * 512:(dm + 1) * 512],
                    in1=xb[:, q * D + dm * 512: q * D + (dm + 1) * 512]))
            s.wait("te", "ve")
        if _tile == TAP_TILE:
            s.tap("ohfmk", s.s_ohfm[:], [64, TT], BF16)
            s.tap("u2k", s.s_attn[:], [128, 2 * D])
        for q in range(2):
            u2q = s.s_attn[:, q * D:(q + 1) * D]
            u2o = s.s_zc[:, 0:D]
            s.ln_tm(u2q, s.s_diff[:, 0:D], 3)
            if _tile == TAP_TILE and q == 0:
                s.tap("ilnm", s.s_mean[:], [128, 4])
                s.tap("ilnr", s.s_rstd[:], [128, 4])
                s.tap("ilnc", s.s_scr[:], [128, 1024])
                s.tap("ilnd", s.s_diff[:, 0:D], [128, 1024])
            s.ve(lambda e: e.tensor_tensor(out=s.s_diff[:, 0:D], in0=s.s_diff[:, 0:D],
                                           in1=s.s_gam[:], op=ALU.mult))
            if _tile == TAP_TILE and q == 0:
                s.tap("gmul", s.s_diff[:, 0:D], [128, 1024])
                s.tap("gamk", s.s_gam[:], [128, 1024])
                s.tap("betk", s.s_bet[:], [128, 1024])
            s.ve(lambda e: e.tensor_add(out=u2o, in0=s.s_diff[:, 0:D], in1=s.s_bet[:]))
            if _tile == TAP_TILE and q == 0:
                s.tap("u2g", u2o, [128, D])
            s.ln_tm(u2o, s.s_diff[:, 0:D], 3)
            s.ve(lambda e, q=q: e.tensor_add(out=xb[:, q * D:(q + 1) * D],
                                             in0=xb[:, q * D:(q + 1) * D], in1=s.s_diff[:, 0:D]))

    def build(self):
        s = self
        self.init_phase()
        s.tap("ccol", s.s_ccol[:], [128, 8])
        s.tap("k0d", s.s_k0[:], [128, 1])
        s.tap("itempd", s.s_itemp[:], [128, 1])
        self.win_marks = {}
        self.next_win_marks = {}
        groups = WIN_GROUPS
        self.fetch_u(0, 0)
        self.win_marks[0] = self.fetch_win(groups[0], 0)
        self.win_marks[1] = self.fetch_win(groups[1], 1)
        umark = s.cnt["dS"]

        for tile in range(NTILE):
            xb = s.s_x[tile % 2]
            s.wait("ve", "dS", umark); s.wait("se", "dS", umark); s.wait("gp", "dS", umark)
            if tile + 1 < NTILE:
                self.fetch_u(tile + 1, (tile + 1) % 2)
                umark = s.cnt["dS"]
            self.tile_ln1(tile, xb)
            if tile == TAP_TILE:
                s.tap("u0", xb[:], [128, 2 * D])
                s.tap("msnap", s.s_msnap[:], [128, 4])
                s.tap("csnap", s.s_csnap[:], [128, 2 * D])
                s.tap("m1", s.s_m1[:], [128, 4])
                s.tap("m2", s.s_m2[:], [128, 4])
                s.tap("meanv", s.s_mean[:], [128, 4])
                s.tap("rstdv", s.s_rstd[:], [128, 4])
            self.tile_inproj(tile)
            self.tile_conv(tile)
            self.tile_dt(tile)
            if tile == TAP_TILE:
                s.tap("ln1tm", s.s_ln[:], [128, 2 * D], BF16)
                s.tap("zfm", s.s_zfm[:], [128, FCX * TT], BF16)
                s.tap("xbc", s.s_xbc[:], [128, FCC * (TT + 3)], BF16)
                s.tap("xfm", s.s_xfm[:], [128, FCX * TT], BF16)
                s.tap("bfm", s.s_bfm[:], [128, TT], BF16)
                s.tap("cfm", s.s_cfm[:], [128, TT], BF16)
                s.tap("dtfm", s.s_dtfm[:], [H, TT])
            for c in range(NCH):
                self.ssm_chunk(tile, c)
                if tile == TAP_TILE and c == 0:
                    s.tap("ca0", s.s_ca0[:], [64, H])
                    s.tap("blob", s.s_blob[:], [128, 16])
                    s.tap("cbt2", s.s_cbt2[:], [128, 64])
                    s.tap("mblob", s.s_mblob[:], [128, 1024], BF16)
                    s.tap("xtm", s.s_xtm[:], [128, 1024], BF16)
                    s.tap("xs", s.s_xs[:], [128, 1024], BF16)
                    s.tap("e2f", s.s_e2f[:], [128, NH])
                    s.tap("e3f", s.s_e3f[:], [128, NH])
                    s.tap("cdec", s.s_cdec[:], [128, H])
                    s.tap("yc", s.s_yc[:], [128, 1024])
                    s.tap("yn", s.s_yn[:], [128, 1024], BF16)
                    s.tap("prev", s.s_prev[:], [128, DI])
                    s.tap("zc", s.s_zc[:], [128, 1024])
                if tile == TAP_TILE and c == 1:
                    s.tap("yc1", s.s_yc[:], [128, 1024])
                    s.tap("prev1", s.s_prev[:], [128, DI])
            if tile == TAP_TILE:
                s.tap("yfm", s.s_yfm[:], [128, 16 * TT], BF16)
            self.tile_outproj(tile, xb)
            if tile == TAP_TILE:
                s.tap("attn", s.s_attn[:], [128, 2 * D])
                s.tap("x2", xb[:], [128, 2 * D])
            self.tile_ffn(tile, xb)
            if tile + 1 < NTILE:
                cur = {"dS": self.cnt["dS"], "dG": self.cnt["dG"]}
                self.win_marks = {0: self.next_win_marks.get(0, cur),
                                  1: self.next_win_marks.get(1, cur)}
            if tile == TAP_TILE:
                s.tap("scores", s.s_scores[:], [128, 2 * 72])
                s.tap("oh", s.s_oh[:], [128, 2 * 64], BF16)
                s.tap("mag", s.s_mag[:], [64, TT])
                s.tap("x3", xb[:], [128, 2 * D])
            s.wait("sy", "ve"); s.wait("sy", "se")
            for q in range(2):
                s.emit_i("sy", "out", 16, lambda eng, tile=tile, q=q, xb=xb: eng.dma_start(
                    out=s.d_out[tile * TT + q * 128: tile * TT + (q + 1) * 128, :],
                    in_=xb[:, q * D:(q + 1) * D]))

        s.wait("sy", "out", s.cnt["out"])
        s.wait("gp", "out", s.cnt["out"])

        nc = self.nc
        with nc.Block() as block:
            @block.sync
            def _(eng):
                for f in self.plan["sy"]:
                    f(eng)

            @block.gpsimd
            def _(eng):
                for f in self.plan["gp"]:
                    f(eng)

            @block.vector
            def _(eng):
                for f in self.plan["ve"]:
                    f(eng)

            @block.scalar
            def _(eng):
                for f in self.plan["se"]:
                    f(eng)

            @block.tensor
            def _(eng):
                for f in self.plan["te"]:
                    f(eng)
        for cm in self._es_sems:
            cm.__exit__(None, None, None)
        return nc


def build_core_program():
    b = Builder()
    nc = b.build()
    return nc, b.tap_outs


def _prep_inputs(inputs):
    i = inputs
    f = np.float32
    dD = np.asarray(i["L0_attn_D"], f)
    dfold = np.concatenate([
        np.broadcast_to(dD[0:16], (64, 16)),
        np.broadcast_to(dD[16:32], (64, 16))], axis=0)
    shared = {
        "winT": np.ascontiguousarray(np.asarray(i["L0_attn_W_in"], f).T),
        "woutT": np.ascontiguousarray(np.asarray(i["L0_attn_W_out"], f).T),
        "convw": np.asarray(i["L0_attn_conv_w"], f),
        "convb": np.ascontiguousarray(np.asarray(i["L0_attn_conv_b"], f).reshape(FCC, 128).T),
        "dtb": np.asarray(i["L0_attn_dt_bias"], f).reshape(H, 1),
        "alog": np.broadcast_to(np.asarray(i["L0_attn_A_log"], f), (128, H)),
        "dfold": dfold,
        "nw": np.ascontiguousarray(np.asarray(i["L0_attn_norm_weight"], f).reshape(16, 128).T),
        "sig": np.asarray(i["L0_ffn_signatures_raw"], f),
        "knots": np.asarray(i["L0_ffn_knot_values"], f).reshape(1, 8),
        "knotsc": np.asarray(i["L0_ffn_knot_values"], f).reshape(8, 1),
        "temp": np.broadcast_to(np.asarray(i["L0_ffn_temperature"], f).reshape(1, 1), (128, 1)),
        "cw1": np.asarray(i["L0_ffn_compress_w1"], f),
        "cb1": np.ascontiguousarray(np.asarray(i["L0_ffn_compress_b1"], f).reshape(2, 128).T),
        "cw2": np.asarray(i["L0_ffn_compress_w2"], f),
        "cb2": np.asarray(i["L0_ffn_compress_b2"], f).reshape(2, 1),
        "w1all": np.ascontiguousarray(
            np.asarray(i["L0_ffn_spline_w1"], f).transpose(1, 0, 2).reshape(2, 1024)),
        "b1all": np.ascontiguousarray(np.asarray(i["L0_ffn_spline_b1"], f).reshape(8, 128).T),
        "b2all": np.asarray(i["L0_ffn_spline_b2"], f).reshape(64, 1),
        "dirs": np.asarray(i["L0_ffn_directions"], f),
        "gam": np.broadcast_to(np.asarray(i["L0_ffn_norm_gamma"], f), (128, D)),
        "bet": np.broadcast_to(np.asarray(i["L0_ffn_norm_beta"], f), (128, D)),
        "os": np.broadcast_to(np.asarray(i["L0_ffn_output_scale"], f).reshape(1, 1), (64, 1)),
    }
    w2bd = np.zeros((1024, 64), f)
    sw2 = np.asarray(i["L0_ffn_spline_w2"], f).reshape(64, 16)
    for t_ in range(64):
        w2bd[t_ * 16:(t_ + 1) * 16, t_] = sw2[t_]
    shared["w2bd"] = w2bd
    shared = {k: np.ascontiguousarray(v, f) for k, v in shared.items()}
    return shared


def kernel(**inputs):
    shared = _prep_inputs(inputs)
    u = np.asarray(inputs["L0_attn_u"], np.float32)
    nc, _ = build_core_program()
    in_maps = []
    for b in range(8):
        m = dict(shared)
        m["u"] = np.ascontiguousarray(u[b])
        in_maps.append(m)
    res = run_bass_kernel_spmd(nc, in_maps, core_ids=list(range(8)))
    out = np.stack([np.asarray(res.results[b]["out"]) for b in range(8)], axis=0)
    return out.astype(np.float32)
